# version 08
# baseline (speedup 1.0000x reference)
import numpy as np
import ml_dtypes
import concourse.bass as bass
import concourse.mybir as mybir
from concourse.bass_utils import run_bass_kernel_spmd
from contextlib import ExitStack

F32 = mybir.dt.float32
F16 = mybir.dt.float16
BF16 = mybir.dt.bfloat16
AF = mybir.ActivationFunctionType
ALU = mybir.AluOpType

P = 128
D = 4096
R = 8192
COLS = 512
SC = D // P
NT = R // P
OWN = NT
M64 = float(1.5 * 2.0 ** 52)
M32 = float(1.5 * 2.0 ** 23)
GQ = 0.5625
EPS = 1e-5
INV256 = 1.0 / 256.0
QMIN, QMAX = -32768.0, 32767.0
PHASES = 3


def build_nc():
    nc = bass.Bass("TRN2", target_bir_lowering=False, detect_race_conditions=False)
    x_own = nc.dram_tensor("x_own", [R, D], F32, kind="ExternalInput")
    x_res = nc.dram_tensor("x_res", [R, COLS], F32, kind="ExternalInput")
    w1 = nc.dram_tensor("w1", [SC, P, D], F32, kind="ExternalInput")
    w2s = nc.dram_tensor("w2s", [D, COLS], F32, kind="ExternalInput")
    masks = nc.dram_tensor("masks", [D, COLS], F32, kind="ExternalInput")
    b2s = nc.dram_tensor("b2s", [1, COLS], F32, kind="ExternalInput")
    wfs = nc.dram_tensor("wfs", [D, COLS], F32, kind="ExternalInput")
    bfs = nc.dram_tensor("bfs", [1, COLS], F32, kind="ExternalInput")
    identb = nc.dram_tensor("identb", [P, P], BF16, kind="ExternalInput")
    identh = nc.dram_tensor("identh", [P, P], F16, kind="ExternalInput")
    onesk = nc.dram_tensor("onesk", [1, P], F32, kind="ExternalInput")
    out = nc.dram_tensor("out", [R, COLS], F32, kind="ExternalOutput")
    tbd = nc.dram_tensor("tbd", [NT, P, D], BF16)
    qd = nc.dram_tensor("qd", [NT, P, D], F16)

    xA = x_own.ap()
    mA = masks.ap()

    es = ExitStack()
    e = es.enter_context
    with es:
        wqT = e(nc.sbuf_tensor("wqT", [P, SC * COLS], F16))
        sidb = e(nc.sbuf_tensor("sidb", [P, P], BF16))
        sidh = e(nc.sbuf_tensor("sidh", [P, P], F16))
        sones = e(nc.sbuf_tensor("sones", [1, P], F32))
        sb2 = e(nc.sbuf_tensor("sb2", [1, COLS], F32))
        sbf = e(nc.sbuf_tensor("sbf", [1, COLS], F32))
        bqrow = e(nc.sbuf_tensor("bqrow", [1, COLS], F32))
        b2rep = e(nc.sbuf_tensor("b2rep", [P, COLS], F32))
        bqrep = e(nc.sbuf_tensor("bqrep", [P, COLS], F32))
        maskb = e(nc.sbuf_tensor("maskb", [P, 2 * COLS], F32))
        vjk = e(nc.sbuf_tensor("vjk", [P, 512], F32))
        pTb = [e(nc.psum_tensor("pT1", [P, 1024], BF16)),
               e(nc.psum_tensor("pT2", [P, 1024], BF16))]
        pTq = [e(nc.psum_tensor("pT3", [P, 1024], F16)),
               e(nc.psum_tensor("pT4", [P, 1024], F16))]
        pM = [e(nc.psum_tensor("pM0", [P, 512], F32)),
              e(nc.psum_tensor("pM1", [P, 512], F32))]
        pF = [e(nc.psum_tensor("pF0", [P, 512], F32)),
              e(nc.psum_tensor("pF1", [P, 512], F32))]
        S = {}
        for nm in ["qsync", "qscal", "qgps", "qvec", "sWQ", "sAF", "sCH",
                   "sTW", "sTQw", "sTBE", "sQE", "sAG", "sP1",
                   "sW2C", "sW1C", "sAMM", "sAE", "sBC", "sBCE", "sP2",
                   "sYMM", "sFMM", "sU", "sEP", "sGELU"]:
            S[nm] = e(nc.semaphore(nm))

        def W(eng, nm, v):
            if v > 0:
                eng.wait_ge(S[nm], v)

        Q1X = 16 * OWN
        Q2M = Q1X + 16 * SC
        QS_INIT = 16 * 6
        QS_WF = QS_INIT + 16 * 4
        QS_W2 = QS_WF + 16 * 4
        QS_W1 = QS_W2 + 16 * SC

        p1 = ExitStack()
        with p1:
            e1 = p1.enter_context
            xt = [e1(nc.sbuf_tensor("xt0", [P, D], F32)),
                  e1(nc.sbuf_tensor("xt1", [P, D], F32))]
            st_ = e1(nc.sbuf_tensor("st_", [P, D], F32))
            tbt = [e1(nc.sbuf_tensor("tbt0", [P, D], BF16)),
                   e1(nc.sbuf_tensor("tbt1", [P, D], BF16))]
            qt = [e1(nc.sbuf_tensor("qt0", [P, D], F16)),
                  e1(nc.sbuf_tensor("qt1", [P, D], F16))]
            HW = 2048
            B1 = e1(nc.sbuf_tensor("B1", [P, HW], F32))
            B2 = e1(nc.sbuf_tensor("B2", [P, HW], F32))
            B3 = e1(nc.sbuf_tensor("B3", [P, HW], F32))
            tbTs = e1(nc.sbuf_tensor("tbTs", [P, D], BF16))
            qTs = e1(nc.sbuf_tensor("qTs", [P, D], F16))
            bn6 = e1(nc.sbuf_tensor("bn6", [P, 96], F32))
            mvA = e1(nc.sbuf_tensor("mvA", [P, 2 * OWN], F32))
            WS = 512
            vew = e1(nc.sbuf_tensor("vew", [P, WS], F32))
            rsw = e1(nc.sbuf_tensor("rsw", [P, WS], F32))
            nww = e1(nc.sbuf_tensor("nww", [P, WS], F32))
            mrsw = e1(nc.sbuf_tensor("mrsw", [P, WS], F32))
            aqw = e1(nc.sbuf_tensor("aqw", [P, WS], F32))
            bqw = e1(nc.sbuf_tensor("bqw", [P, WS], F32))

            blk = p1.enter_context(nc.Block())

            @blk.sync
            def _(sync):
                for i in range(OWN):
                    W(sync, "sAF", i - 1)
                    sync.dma_start(xt[i % 2][:], xA[i * P:(i + 1) * P, :]).then_inc(S["qsync"], 16)
                sync.sem_inc(S["sP1"], 1)

            @blk.scalar
            def _(scalar):
                scalar.dma_start(sidb[:], identb.ap()).then_inc(S["qscal"], 16)
                scalar.dma_start(sidh[:], identh.ap()).then_inc(S["qscal"], 16)
                scalar.dma_start(sones[:], onesk.ap()).then_inc(S["qscal"], 16)
                scalar.dma_start(sb2[:], b2s.ap()).then_inc(S["qscal"], 16)
                scalar.dma_start(sbf[:], bfs.ap()).then_inc(S["qscal"], 16)
                scalar.dma_start(sbf[:], bfs.ap()).then_inc(S["qscal"], 16)
                for k in range(4):
                    W(scalar, "sWQ", k)
                    scalar.dma_start(st_[:],
                                     bass.AP(wfs, k * 1024 * COLS,
                                             [[COLS, P], [P * COLS, 8], [1, COLS]])).then_inc(S["qscal"], 16)

            @blk.vector
            def _(vector):
                def rne_chain(src, dst, rows, width, clip):
                    b1 = B1[0:rows, 0:width]
                    b2_ = B2[0:rows, 0:width]
                    b3 = B3[0:rows, 0:width]
                    if clip:
                        vector.tensor_scalar(b1, src, 256.0, QMIN, op0=ALU.mult, op1=ALU.max)
                        vector.tensor_scalar(b2_, b1, QMAX, None, op0=ALU.min)
                        src = b2_
                    vector.tensor_scalar(b3, src, M64, -M64, op0=ALU.add, op1=ALU.add)
                    vector.scalar_tensor_tensor(b1, src, M32, b3, op0=ALU.add, op1=ALU.subtract)
                    vector.tensor_scalar(b2_, b1, -M32, None, op0=ALU.add)
                    vector.tensor_tensor(b1, b2_, b2_, op=ALU.mult)
                    vector.tensor_scalar(b1, b1, GQ, None, op0=ALU.is_gt)
                    vector.tensor_tensor(b2_, b2_, b1, op=ALU.mult)
                    vector.tensor_tensor(dst, b3, b2_, op=ALU.add)

                for k in range(4):
                    W(vector, "qscal", QS_INIT + 16 * (k + 1))
                    for c in range(8):
                        jc = 8 * k + c
                        rne_chain(st_[:, c * COLS:(c + 1) * COLS],
                                  wqT[:, jc * COLS:(jc + 1) * COLS], P, COLS, True)
                    vector.tensor_copy(vjk[:, 0:256], vjk[:, 256:512]).then_inc(S["sWQ"], 1)
                W(vector, "qscal", QS_INIT)
                rne_chain(sbf[0:1, :], bqrow[0:1, :], 1, COLS, True)
                vector.tensor_copy(vjk[:, 0:256], vjk[:, 256:512]).then_inc(S["sWQ"], 1)

                for i in range(OWN):
                    W(vector, "qsync", 16 * (i + 1))
                    for c in range(16):
                        vector.bn_stats(bn6[:, 6 * c:6 * (c + 1)], xt[i % 2][:, 256 * c:256 * (c + 1)])
                    vector.bn_aggr(mvA[:, 2 * i:2 * i + 2], bn6[:])
                    vector.tensor_copy(vjk[:, 0:256], vjk[:, 256:512])
                    vector.tensor_scalar(vew[:], bass.AP(mvA, 2 * i + 1, [[2 * OWN, P], [0, WS]]),
                                         EPS, None, op0=ALU.add)
                    vector.tensor_scalar(rsw[:], vew[:], -0.5, 1.5, op0=ALU.mult, op1=ALU.add)
                    for _ in range(3):
                        vector.tensor_tensor(nww[:], rsw[:], rsw[:], op=ALU.mult)
                        vector.tensor_tensor(nww[:], nww[:], vew[:], op=ALU.mult)
                        vector.tensor_scalar(nww[:], nww[:], -0.5, 1.5, op0=ALU.mult, op1=ALU.add)
                        vector.tensor_tensor(rsw[:], rsw[:], nww[:], op=ALU.mult)
                    vector.tensor_tensor(mrsw[:], bass.AP(mvA, 2 * i, [[2 * OWN, P], [0, WS]]),
                                         rsw[:], op=ALU.mult)
                    vector.tensor_scalar(aqw[:], rsw[:], 256.0, None, op0=ALU.mult)
                    vector.tensor_scalar(bqw[:], mrsw[:], 256.0, None, op0=ALU.mult)
                    W(vector, "sTW", 4 * (i - 1))
                    vector.tensor_scalar(tbt[i % 2][:], xt[i % 2][:], rsw[:, 0:1], mrsw[:, 0:1],
                                         op0=ALU.mult, op1=ALU.subtract)
                    vector.tensor_scalar(st_[:], xt[i % 2][:], aqw[:, 0:1], bqw[:, 0:1],
                                         op0=ALU.mult, op1=ALU.subtract)
                    vector.tensor_copy(vjk[:, 0:256], vjk[:, 256:512]).then_inc(S["sAF"], 1)
                    W(vector, "sTQw", 4 * (i - 1))
                    for h in range(2):
                        rne_chain(st_[:, h * HW:(h + 1) * HW],
                                  qt[i % 2][:, h * HW:(h + 1) * HW], P, HW, False)
                    vector.tensor_copy(vjk[:, 0:256], vjk[:, 256:512]).then_inc(S["sCH"], 1)
                    for w in range(4):
                        idx = 4 * i + w
                        W(vector, "sTW", idx + 1)
                        if w == 0 and i > 0:
                            W(vector, "qgps", 32 * (i - 1) + 16)
                        vector.tensor_copy(tbTs[:, w * 1024:(w + 1) * 1024], pTb[idx % 2][:])
                        vector.tensor_copy(vjk[:, 0:256], vjk[:, 256:512]).then_inc(S["sTBE"], 1)
                    for w in range(4):
                        idx = 4 * i + w
                        W(vector, "sTQw", idx + 1)
                        if w == 0 and i > 0:
                            W(vector, "qgps", 32 * i)
                        vector.tensor_copy(qTs[:, w * 1024:(w + 1) * 1024], pTq[idx % 2][:])
                        vector.tensor_copy(vjk[:, 0:256], vjk[:, 256:512]).then_inc(S["sQE"], 1)
                vector.sem_inc(S["sP1"], 1)

            @blk.tensor
            def _(tensor):
                for i in range(OWN):
                    W(tensor, "sAF", i + 1)
                    for w in range(4):
                        idx = 4 * i + w
                        W(tensor, "sTBE", idx - 1)
                        for b in range(8):
                            sc = 8 * w + b
                            tt = tensor.transpose(pTb[idx % 2][:, b * P:(b + 1) * P],
                                                  tbt[i % 2][:, sc * P:(sc + 1) * P], sidb[:])
                        tt.then_inc(S["sTW"], 1)
                    W(tensor, "sCH", i + 1)
                    for w in range(4):
                        idx = 4 * i + w
                        W(tensor, "sQE", idx - 1)
                        for b in range(8):
                            sc = 8 * w + b
                            tt = tensor.transpose(pTq[idx % 2][:, b * P:(b + 1) * P],
                                                  qt[i % 2][:, sc * P:(sc + 1) * P], sidh[:])
                        tt.then_inc(S["sTQw"], 1)

            @blk.gpsimd
            def _(gpsimd):
                for i in range(OWN):
                    W(gpsimd, "sTBE", 4 * (i + 1))
                    gpsimd.dma_start(bass.AP(tbd, i * P * D, [[D, P], [1, D]]),
                                     tbTs[:]).then_inc(S["qgps"], 16)
                    W(gpsimd, "sQE", 4 * (i + 1))
                    gpsimd.dma_start(bass.AP(qd, i * P * D, [[D, P], [1, D]]),
                                     qTs[:]).then_inc(S["qgps"], 16)
                    gpsimd.wait_ge(S["qgps"], 32 * (i + 1))
                    gpsimd.sem_inc(S["sAG"], 1)
                gpsimd.sem_inc(S["sP1"], 1)

        if PHASES == 1:
            dbg_tb = nc.dram_tensor("dbg_tb", [NT, P, D], BF16, kind="ExternalOutput")
            dbg_q = nc.dram_tensor("dbg_q", [NT, P, D], F16, kind="ExternalOutput")
            dbg_wq = nc.dram_tensor("dbg_wq", [P, SC * COLS], F16, kind="ExternalOutput")
            dbg_rs = nc.dram_tensor("dbg_rs", [P, 2 * OWN], F32, kind="ExternalOutput")
            with nc.Block() as blkz:
                @blkz.gpsimd
                def _(gpsimd):
                    W(gpsimd, "sP1", 3)
                    gpsimd.dma_start(dbg_tb.ap(), tbd.ap()).then_inc(S["qvec"], 16)
                    gpsimd.dma_start(dbg_q.ap(), qd.ap()).then_inc(S["qvec"], 16)
                    gpsimd.dma_start(dbg_wq.ap(), wqT[:]).then_inc(S["qvec"], 16)
                    gpsimd.dma_start(bass.AP(dbg_rs, 0, [[2 * OWN, P], [1, OWN]]),
                                     bass.AP(mvA, 0, [[2 * OWN, P], [1, OWN]])).then_inc(S["qvec"], 16)
                    gpsimd.dma_start(bass.AP(out, 0, [[COLS, P], [1, COLS]]),
                                     b2rep[:]).then_inc(S["qvec"], 16)
                    gpsimd.wait_ge(S["qvec"], 80)
            return nc

        p23 = ExitStack()
        with p23:
            attn = p23.enter_context(nc.sbuf_tensor("attn", [P, SC * COLS], BF16))

            p2 = ExitStack()
            with p2:
                e2 = p2.enter_context
                w1st = [e2(nc.sbuf_tensor("w1st0", [P, D], F32)),
                        e2(nc.sbuf_tensor("w1st1", [P, D], F32))]
                w1c = [e2(nc.sbuf_tensor("w1c0", [P, D], BF16)),
                       e2(nc.sbuf_tensor("w1c1", [P, D], BF16))]
                w2st = e2(nc.sbuf_tensor("w2st", [P, D], F32))
                w2T = e2(nc.sbuf_tensor("w2T", [P, SC * COLS], BF16))

                blk2 = p2.enter_context(nc.Block())

                @blk2.sync
                def _(sync):
                    W(sync, "sP1", 3)
                    for k in range(SC):
                        W(sync, "sAE", k - 1)
                        sync.dma_start(maskb[:, (k % 2) * COLS:(k % 2 + 1) * COLS],
                                       mA[k * P:(k + 1) * P, :]).then_inc(S["qsync"], 16)
                    sync.sem_inc(S["sP2"], 1)

                @blk2.scalar
                def _(scalar):
                    W(scalar, "sP1", 3)
                    for tc in range(4):
                        W(scalar, "sW2C", tc)
                        scalar.dma_start(w2st[:],
                                         bass.AP(w2s, tc * 1024 * COLS,
                                                 [[COLS, P], [P * COLS, 8], [1, COLS]])).then_inc(S["qscal"], 16)
                    for k in range(SC):
                        W(scalar, "sW1C", k - 1)
                        scalar.dma_start(w1st[k % 2][:],
                                         bass.AP(w1, k * P * D, [[D, P], [1, D]])).then_inc(S["qscal"], 16)

                @blk2.vector
                def _(vector):
                    W(vector, "sP1", 3)
                    for tc in range(4):
                        vector.wait_ge(S["qscal"], QS_WF + 16 * (tc + 1))
                        vector.tensor_copy(w2T[:, tc * 4096:(tc + 1) * 4096], w2st[:])
                        vector.tensor_copy(vjk[:, 0:256], vjk[:, 256:512]).then_inc(S["sW2C"], 1)
                    W(vector, "sBC", 1)
                    vector.tensor_copy(b2rep[:], pF[0][:])
                    W(vector, "sBC", 2)
                    vector.tensor_copy(bqrep[:], pF[1][:])
                    vector.tensor_copy(vjk[:, 0:256], vjk[:, 256:512]).then_inc(S["sBCE"], 1)
                    for k in range(SC):
                        vector.wait_ge(S["qscal"], QS_W2 + 16 * (k + 1))
                        W(vector, "sAMM", k - 1)
                        vector.tensor_copy(w1c[k % 2][:], w1st[k % 2][:])
                        vector.tensor_copy(vjk[:, 0:256], vjk[:, 256:512]).then_inc(S["sW1C"], 1)
                        W(vector, "sAMM", k + 1)
                        vector.wait_ge(S["qsync"], Q1X + 16 * (k + 1))
                        vector.tensor_tensor(attn[:, k * COLS:(k + 1) * COLS], pM[k % 2][:],
                                             maskb[:, (k % 2) * COLS:(k % 2 + 1) * COLS],
                                             op=ALU.mult).then_inc(S["sAE"], 1)
                    vector.tensor_copy(vjk[:, 0:256], vjk[:, 256:512]).then_inc(S["sP2"], 1)

                @blk2.tensor
                def _(tensor):
                    W(tensor, "sP1", 3)
                    tensor.matmul(pF[0][:], sones[0:1, :], sb2[0:1, :],
                                  start=True, stop=True).then_inc(S["sBC"], 1)
                    tensor.matmul(pF[1][:], sones[0:1, :], bqrow[0:1, :],
                                  start=True, stop=True).then_inc(S["sBC"], 1)
                    for k in range(SC):
                        W(tensor, "sW1C", k + 1)
                        W(tensor, "sAE", k - 1)
                        for dc in range(SC):
                            mm = tensor.matmul(pM[k % 2][:], w1c[k % 2][:, dc * P:(dc + 1) * P],
                                               w2T[:, dc * COLS:(dc + 1) * COLS],
                                               start=(dc == 0), stop=(dc == SC - 1))
                        mm.then_inc(S["sAMM"], 1)

            if PHASES == 2:
                dbg_at = nc.dram_tensor("dbg_at", [P, SC * COLS], BF16, kind="ExternalOutput")
                dbg_b2 = nc.dram_tensor("dbg_b2", [P, 2 * COLS], F32, kind="ExternalOutput")
                with nc.Block() as blkz:
                    @blkz.gpsimd
                    def _(gpsimd):
                        W(gpsimd, "sP2", 2)
                        gpsimd.dma_start(dbg_at.ap(), attn[:]).then_inc(S["qvec"], 16)
                        gpsimd.dma_start(bass.AP(dbg_b2, 0, [[2 * COLS, P], [1, COLS]]),
                                         b2rep[:]).then_inc(S["qvec"], 16)
                        gpsimd.dma_start(bass.AP(dbg_b2, COLS, [[2 * COLS, P], [1, COLS]]),
                                         bqrep[:]).then_inc(S["qvec"], 16)
                        gpsimd.dma_start(bass.AP(out, 0, [[COLS, P], [1, COLS]]),
                                         b2rep[:]).then_inc(S["qvec"], 16)
                        gpsimd.wait_ge(S["qvec"], 64)
                return nc

            p3 = ExitStack()
            with p3:
                e3 = p3.enter_context
                tbl = [e3(nc.sbuf_tensor("tbl0", [P, D], BF16)),
                       e3(nc.sbuf_tensor("tbl1", [P, D], BF16))]
                ql = [e3(nc.sbuf_tensor("ql0", [P, D], F16)),
                      e3(nc.sbuf_tensor("ql1", [P, D], F16))]
                xrs = [e3(nc.sbuf_tensor("xrs0", [P, COLS], F32)),
                       e3(nc.sbuf_tensor("xrs1", [P, COLS], F32))]
                usb = [e3(nc.sbuf_tensor("usb0", [P, COLS], F32)),
                       e3(nc.sbuf_tensor("usb1", [P, COLS], F32))]
                gts = [e3(nc.sbuf_tensor("gts0", [P, COLS], F32)),
                       e3(nc.sbuf_tensor("gts1", [P, COLS], F32))]
                osb = [e3(nc.sbuf_tensor("osb0", [P, COLS], F32)),
                       e3(nc.sbuf_tensor("osb1", [P, COLS], F32))]
                E1 = e3(nc.sbuf_tensor("E1", [P, COLS], F32))
                E2 = e3(nc.sbuf_tensor("E2", [P, COLS], F32))
                E3 = e3(nc.sbuf_tensor("E3", [P, COLS], F32))
                E4 = e3(nc.sbuf_tensor("E4", [P, COLS], F32))

                blk3 = p3.enter_context(nc.Block())

                @blk3.sync
                def _(sync):
                    W(sync, "sP2", 2)
                    for g in range(NT):
                        W(sync, "sAG", g + 1)
                        W(sync, "sFMM", g - 1)
                        sync.dma_start(tbl[g % 2][:],
                                       bass.AP(tbd, g * P * D, [[D, P], [1, D]])).then_inc(S["qsync"], 16)
                        sync.dma_start(ql[g % 2][:],
                                       bass.AP(qd, g * P * D, [[D, P], [1, D]])).then_inc(S["qsync"], 16)

                @blk3.scalar
                def _(scalar):
                    W(scalar, "sP2", 2)
                    for idx in range(NT):
                        W(scalar, "sEP", idx - 1)
                        scalar.dma_start(xrs[idx % 2][:],
                                         bass.AP(x_res, idx * P * COLS,
                                                 [[COLS, P], [1, COLS]])).then_inc(S["qscal"], 16)
                        W(scalar, "sU", idx + 1)
                        W(scalar, "sEP", idx - 1)
                        scalar.activation(gts[idx % 2][:], usb[idx % 2][:], AF.Gelu)
                        scalar.mul(gts[idx % 2][:, 0:1], gts[idx % 2][:, 0:1],
                                   1.0).then_inc(S["sGELU"], 1)

                @blk3.tensor
                def _(tensor):
                    W(tensor, "sP2", 2)
                    for idx in range(NT):
                        tensor.wait_ge(S["qsync"], Q2M + 32 * idx + 16)
                        W(tensor, "sU", idx - 1)
                        for sc in range(SC):
                            mm = tensor.matmul(pM[idx % 2][:],
                                               tbl[idx % 2][:, sc * P:(sc + 1) * P],
                                               attn[:, sc * COLS:(sc + 1) * COLS],
                                               start=(sc == 0), stop=(sc == SC - 1))
                        mm.then_inc(S["sYMM"], 1)
                        tensor.wait_ge(S["qsync"], Q2M + 32 * idx + 32)
                        W(tensor, "sEP", idx - 1)
                        for sc in range(SC):
                            mm = tensor.matmul(pF[idx % 2][:],
                                               ql[idx % 2][:, sc * P:(sc + 1) * P],
                                               wqT[:, sc * COLS:(sc + 1) * COLS],
                                               start=(sc == 0), stop=(sc == SC - 1))
                        mm.then_inc(S["sFMM"], 1)

                @blk3.vector
                def _(vector):
                    W(vector, "sP2", 2)
                    W(vector, "sBCE", 1)
                    for idx in range(NT):
                        W(vector, "sYMM", idx + 1)
                        W(vector, "sGELU", idx - 1)
                        vector.scalar_tensor_tensor(usb[idx % 2][:], pM[idx % 2][:], 0.0, b2rep[:],
                                                    op0=ALU.bypass, op1=ALU.add)
                        vector.tensor_copy(vjk[:, 0:256], vjk[:, 256:512]).then_inc(S["sU"], 1)
                        W(vector, "sFMM", idx + 1)
                        vector.tensor_scalar(E1[:], pF[idx % 2][:], INV256, None, op0=ALU.mult)
                        vector.tensor_scalar(E2[:], E1[:], M64, -M64, op0=ALU.add, op1=ALU.add)
                        vector.scalar_tensor_tensor(E3[:], E1[:], M32, E2[:], op0=ALU.add, op1=ALU.subtract)
                        vector.tensor_scalar(E3[:], E3[:], -M32, None, op0=ALU.add)
                        vector.tensor_tensor(E4[:], E3[:], E3[:], op=ALU.mult)
                        vector.tensor_scalar(E4[:], E4[:], GQ, None, op0=ALU.is_gt)
                        vector.tensor_tensor(E3[:], E3[:], E4[:], op=ALU.mult)
                        vector.tensor_tensor(E2[:], E2[:], E3[:], op=ALU.add)
                        vector.tensor_tensor(E3[:], E1[:], E2[:], op=ALU.is_lt)
                        vector.tensor_tensor(E1[:], E2[:], E3[:], op=ALU.subtract)
                        vector.tensor_scalar(E2[:], E1[:], QMIN, QMAX, op0=ALU.max, op1=ALU.min)
                        vector.scalar_tensor_tensor(E1[:], E2[:], 0.0, bqrep[:], op0=ALU.bypass, op1=ALU.add)
                        vector.tensor_scalar(E2[:], E1[:], QMIN, QMAX, op0=ALU.max, op1=ALU.min)
                        vector.wait_ge(S["qscal"], QS_W1 + 16 * (idx + 1))
                        vector.scalar_tensor_tensor(E1[:], E2[:], INV256, xrs[idx % 2][:],
                                                    op0=ALU.mult, op1=ALU.add)
                        W(vector, "sGELU", idx + 1)
                        if idx > 1:
                            vector.wait_ge(S["qvec"], 16 * (idx - 1))
                        vector.tensor_tensor(osb[idx % 2][:], E1[:], gts[idx % 2][:], op=ALU.add)
                        vector.tensor_copy(vjk[:, 0:256], vjk[:, 256:512]).then_inc(S["sEP"], 1)
                    vector.wait_ge(S["qvec"], 16 * NT)

                @blk3.gpsimd
                def _(gpsimd):
                    for idx in range(NT):
                        W(gpsimd, "sEP", idx + 1)
                        gpsimd.dma_start(bass.AP(out, idx * P * COLS, [[COLS, P], [1, COLS]]),
                                         osb[idx % 2][:]).then_inc(S["qvec"], 16)
                    gpsimd.wait_ge(S["qvec"], 16 * NT)

    return nc


def kernel(L0_attn_x, L0_attn_w1, L0_attn_w2, L0_attn_b2, L0_attn_sparse_mask,
           L0_ffn_weight, L0_ffn_bias):
    x = np.ascontiguousarray(np.asarray(L0_attn_x, dtype=np.float32).reshape(R, D))
    w1 = np.asarray(L0_attn_w1, dtype=np.float32)
    w1r = np.ascontiguousarray(
        w1.reshape(SC, P, SC, P).transpose(2, 1, 0, 3).reshape(SC, P, D))
    w2 = np.ascontiguousarray(np.asarray(L0_attn_w2, dtype=np.float32))
    b2 = np.asarray(L0_attn_b2, dtype=np.float32)
    mask = np.ascontiguousarray(np.asarray(L0_attn_sparse_mask, dtype=np.float32))
    wf = np.ascontiguousarray(np.asarray(L0_ffn_weight, dtype=np.float32))
    bf = np.asarray(L0_ffn_bias, dtype=np.float32)

    nc = build_nc()
    in_maps = []
    for c in range(8):
        cs = slice(c * COLS, (c + 1) * COLS)
        in_maps.append({
            "x_own": x,
            "x_res": np.ascontiguousarray(x[:, cs]),
            "w1": w1r,
            "w2s": np.ascontiguousarray(w2[cs, :].T),
            "masks": np.ascontiguousarray(mask[:, cs]),
            "b2s": np.ascontiguousarray(b2[cs].reshape(1, COLS)),
            "wfs": np.ascontiguousarray(wf[cs, :].T),
            "bfs": np.ascontiguousarray(bf[cs].reshape(1, COLS)),
            "identb": np.eye(P, dtype=ml_dtypes.bfloat16),
            "identh": np.eye(P, dtype=np.float16),
            "onesk": np.ones((1, P), dtype=np.float32),
        })
    res = run_bass_kernel_spmd(nc, in_maps, core_ids=list(range(8)))
    outs = [np.asarray(r["out"], dtype=np.float32) for r in res.results]
    full = np.concatenate(outs, axis=1)
    return full.reshape(2, 4096, 4096)


# version 10
# speedup vs baseline: 1.3288x; 1.3288x over previous
import numpy as np
import ml_dtypes
import concourse.bass as bass
import concourse.mybir as mybir
from concourse.bass_utils import run_bass_kernel_spmd
from contextlib import ExitStack

F32 = mybir.dt.float32
F16 = mybir.dt.float16
BF16 = mybir.dt.bfloat16
AF = mybir.ActivationFunctionType
ALU = mybir.AluOpType

P = 128
D = 4096
R = 8192
COLS = 512
SC = D // P
NT = R // P
QW = 1024
XR = 6
WV = 2304
WG = D - WV
HV = WV // 2
KOFF = 1
M64 = float(1.5 * 2.0 ** 52)
M32 = float(1.5 * 2.0 ** 23)
EPS = 1e-5
INV256 = 1.0 / 256.0
QMIN, QMAX = -32768.0, 32767.0


def build_nc():
    nc = bass.Bass("TRN2", target_bir_lowering=False, detect_race_conditions=False)
    x_own = nc.dram_tensor("x_own", [R, D], F32, kind="ExternalInput")
    w1 = nc.dram_tensor("w1", [SC, P, D], F32, kind="ExternalInput")
    w2s = nc.dram_tensor("w2s", [D, COLS], F32, kind="ExternalInput")
    masks = nc.dram_tensor("masks", [D, COLS], F32, kind="ExternalInput")
    b2s = nc.dram_tensor("b2s", [1, COLS], F32, kind="ExternalInput")
    wfs = nc.dram_tensor("wfs", [D, COLS], F32, kind="ExternalInput")
    bfs = nc.dram_tensor("bfs", [1, COLS], F32, kind="ExternalInput")
    identb = nc.dram_tensor("identb", [P, P], BF16, kind="ExternalInput")
    identh = nc.dram_tensor("identh", [P, P], F16, kind="ExternalInput")
    onesk = nc.dram_tensor("onesk", [1, P], F32, kind="ExternalInput")
    onesb = nc.dram_tensor("onesb", [1, P], BF16, kind="ExternalInput")
    csd = nc.dram_tensor("csd", [P, 8], F32, kind="ExternalInput")
    out = nc.dram_tensor("out", [R, COLS], F32, kind="ExternalOutput")

    es = ExitStack()
    e = es.enter_context
    with es:
        wqT = e(nc.sbuf_tensor("wqT", [P, SC * COLS], F16))
        attn = e(nc.sbuf_tensor("attn", [P, SC * COLS], BF16))
        bqrep = e(nc.sbuf_tensor("bqrep", [P, COLS], F32))
        sidb = e(nc.sbuf_tensor("sidb", [P, P], BF16))
        sidh = e(nc.sbuf_tensor("sidh", [P, P], F16))
        b2b = e(nc.sbuf_tensor("b2b", [1, COLS], BF16))
        onb = e(nc.sbuf_tensor("onb", [1, P], BF16))
        cs = e(nc.sbuf_tensor("cs", [P, 8], F32))
        phiw = e(nc.sbuf_tensor("phiw", [P, 64], F32))
        vjk = e(nc.sbuf_tensor("vjk", [P, 64], F32))
        gjk = e(nc.sbuf_tensor("gjk", [P, 64], F32))
        bn6 = e(nc.sbuf_tensor("bn6", [P, 96], F32))
        mv = e(nc.sbuf_tensor("mv", [P, 2], F32))
        vew = e(nc.sbuf_tensor("vew", [P, 64], F32))
        rsw = e(nc.sbuf_tensor("rsw", [P, 128], F32))
        nww = e(nc.sbuf_tensor("nww", [P, 64], F32))
        nmp = e(nc.sbuf_tensor("nmp", [P, 64], F32))
        nmw = e(nc.sbuf_tensor("nmw", [P, 128], F32))
        aqw = e(nc.sbuf_tensor("aqw", [P, 64], F32))
        nbw = e(nc.sbuf_tensor("nbw", [P, 64], F32))
        E1 = e(nc.sbuf_tensor("E1", [P, COLS], F32))
        E2 = e(nc.sbuf_tensor("E2", [P, COLS], F32))
        E3 = e(nc.sbuf_tensor("E3", [P, COLS], F32))
        xrs = [e(nc.sbuf_tensor("xrs0", [P, COLS], F32)),
               e(nc.sbuf_tensor("xrs1", [P, COLS], F32))]

        pTb = [e(nc.psum_tensor("pT1", [P, 1024], BF16)),
               e(nc.psum_tensor("pT2", [P, 1024], BF16))]
        pTq = [e(nc.psum_tensor("pT3", [P, 1024], F16)),
               e(nc.psum_tensor("pT4", [P, 1024], F16))]
        pM = [e(nc.psum_tensor("pM0", [P, 512], F32)),
              e(nc.psum_tensor("pM1", [P, 512], F32))]
        pF = [e(nc.psum_tensor("pF0", [P, 512], F32)),
              e(nc.psum_tensor("pF1", [P, 512], F32))]

        S = {}
        for nm in ["qsync", "qscal", "qgps", "qvec",
                   "sW1C", "sW2C", "sBQR", "sBC", "sBCE", "sWQC", "sWQG",
                   "sAMM", "sAE", "sST", "sTB", "sQV", "sQG",
                   "sTW", "sTQw", "sTBE", "sQE",
                   "sYMM", "sFMM", "sGELU", "sEP"]:
            S[nm] = e(nc.semaphore(nm))

        def W(eng, nm, v):
            if v > 0:
                eng.wait_ge(S[nm], v)

        def cb(col, w):
            return bass.AP(cs, col, [[8, P], [0, w]])

        def phib(w):
            return bass.AP(phiw, 0, [[64, P], [0, w]])

        sync_plan = []
        for k in range(1, SC, 2):
            sync_plan.append(("w1", k, 0))
            sync_plan.append(("w1", k, 1))
        for j in (0, 1):
            for qq in range(4):
                sync_plan.append(("x", j, qq))
        sync_plan.append(("xr", 0))
        for jx in range(2, NT):
            for qq in range(4):
                sync_plan.append(("x", jx, qq))
            sync_plan.append(("xr", jx - 1))
        sync_plan.append(("xr", NT - 1))
        QS = {}
        for n, tag in enumerate(sync_plan):
            QS[tag] = 16 * (n + 1)

        scal_plan = ["sidb", "sidh", "cs", "onesk", "onesb", "b2s", "bfs"]
        for hc in range(16):
            scal_plan.append(("w2", hc))
        for k in range(0, SC, 2):
            for hh in range(2):
                scal_plan.append(("w1", k, hh))
            if k + 1 < SC:
                scal_plan.append(("mask", k))
                scal_plan.append(("mask", k + 1))
        QL = {}
        for n, tag in enumerate(scal_plan):
            QL[tag] = 16 * (n + 1)

        QG_WF = {c: 16 * (c + 1) for c in range(16)}

        gA = []

        def aA(*args):
            g = nc.sbuf_tensor(*args)
            gA.append(g)
            return g.__enter__()

        sones = aA("sones", [1, P], F32)
        sb2 = aA("sb2", [1, COLS], F32)
        sbf = aA("sbf", [1, COLS], F32)
        bqrow = aA("bqrow", [1, COLS], F32)
        w2st = [aA("w2st0", [P, 1024], F32), aA("w2st1", [P, 1024], F32)]
        w2T = aA("w2T", [P, SC * COLS], BF16)
        w1stA = [aA("w1stA0", [P, 2048], F32), aA("w1stA1", [P, 2048], F32)]
        w1stB = [aA("w1stB0", [P, 2048], F32), aA("w1stB1", [P, 2048], F32)]
        w1c = [aA("w1c0", [P, D], BF16), aA("w1c1", [P, D], BF16)]
        wfst = [aA("wfst0", [P, 1024], F32), aA("wfst1", [P, 1024], F32)]
        wfcl = [aA("wfcl0", [P, 1024], F32), aA("wfcl1", [P, 1024], F32)]
        wqs1 = aA("wqs1", [P, COLS], F32)
        wqs2 = aA("wqs2", [P, COLS], F32)
        maskb = aA("maskb", [P, 4 * COLS], F32)
        for g in reversed(gA):
            g.__exit__(None, None, None)

        p2 = ExitStack()
        with p2:
            e2 = p2.enter_context
            xq = [e2(nc.sbuf_tensor(f"xq{b}", [P, QW], F32)) for b in range(XR)]
            stV = e2(nc.sbuf_tensor("stV", [P, WV], F32))
            stG = [e2(nc.sbuf_tensor("stG0", [P, WG], F32)),
                   e2(nc.sbuf_tensor("stG1", [P, WG], F32))]
            tbt = e2(nc.sbuf_tensor("tbt", [P, D], BF16))
            qt = e2(nc.sbuf_tensor("qt", [P, D], F16))
            vc1 = e2(nc.sbuf_tensor("vc1", [P, HV], F32))
            vc2 = e2(nc.sbuf_tensor("vc2", [P, HV], F32))
            gc1 = e2(nc.sbuf_tensor("gc1", [P, WG], F32))
            gc2 = e2(nc.sbuf_tensor("gc2", [P, WG], F32))
            tbf = [e2(nc.sbuf_tensor("tbf0", [P, D], BF16)),
                   e2(nc.sbuf_tensor("tbf1", [P, D], BF16))]
            qf = [e2(nc.sbuf_tensor("qf0", [P, D], F16)),
                  e2(nc.sbuf_tensor("qf1", [P, D], F16))]
            gts = e2(nc.sbuf_tensor("gts", [P, COLS], F32))
            osb = [e2(nc.sbuf_tensor("osb0", [P, COLS], F32)),
                   e2(nc.sbuf_tensor("osb1", [P, COLS], F32))]

            blk = p2.enter_context(nc.Block())

            @blk.sync
            def _(sync):
                for tag in sync_plan:
                    if tag[0] == "x":
                        _, j, qq = tag
                        hx = 4 * j + qq
                        if hx == 0:
                            W(sync, "sW1C", SC)
                            W(sync, "sAE", SC)
                            W(sync, "sWQG", 16)
                        prev = hx - XR
                        if prev >= 0:
                            jp = prev // 4
                            W(sync, "sST", jp + 1)
                            W(sync, "sTB", jp + 1)
                        sync.dma_start(
                            xq[hx % XR][:],
                            bass.AP(x_own, j * P * D + qq * QW, [[D, P], [1, QW]])
                        ).then_inc(S["qsync"], 16)
                    elif tag[0] == "w1":
                        _, k, hh = tag
                        if k >= 3:
                            W(sync, "sW1C", k - 1)
                        sync.dma_start(
                            w1stB[hh][:],
                            bass.AP(w1, k * P * D + hh * 2048, [[D, P], [1, 2048]])
                        ).then_inc(S["qsync"], 16)
                    else:
                        _, i = tag
                        W(sync, "sEP", i - 1)
                        sync.dma_start(
                            xrs[i % 2][:],
                            bass.AP(x_own, i * P * D, [[D, P], [1, COLS]])
                        ).then_inc(S["qsync"], 16)

            @blk.scalar
            def _(scalar):
                scalar.dma_start(sidb[:], identb.ap()).then_inc(S["qscal"], 16)
                scalar.dma_start(sidh[:], identh.ap()).then_inc(S["qscal"], 16)
                scalar.dma_start(cs[:], csd.ap()).then_inc(S["qscal"], 16)
                scalar.dma_start(sones[:], onesk.ap()).then_inc(S["qscal"], 16)
                scalar.dma_start(onb[:], onesb.ap()).then_inc(S["qscal"], 16)
                scalar.dma_start(sb2[:], b2s.ap()).then_inc(S["qscal"], 16)
                scalar.dma_start(sbf[:], bfs.ap()).then_inc(S["qscal"], 16)
                for hc in range(16):
                    scalar.dma_start(
                        w2st[hc % 2][:],
                        bass.AP(w2s, hc * 256 * COLS, [[COLS, P], [P * COLS, 2], [1, COLS]])
                    ).then_inc(S["qscal"], 16)
                    scalar.wait_ge(S["qscal"], QL[("w2", hc)])
                    scalar.copy(w2T[:, hc * 1024:(hc + 1) * 1024], w2st[hc % 2][:])
                scalar.mul(w2T[:, 0:1], w2T[:, 0:1], 1.0).then_inc(S["sW2C"], 1)
                for hh in range(2):
                    scalar.dma_start(
                        w1stA[hh][:],
                        bass.AP(w1, 0 * P * D + hh * 2048, [[D, P], [1, 2048]])
                    ).then_inc(S["qscal"], 16)
                for k in range(SC):
                    for hh in range(2):
                        if k % 2 == 0:
                            scalar.wait_ge(S["qscal"], QL[("w1", k, hh)])
                            src = w1stA[hh]
                        else:
                            scalar.wait_ge(S["qsync"], QS[("w1", k, hh)])
                            src = w1stB[hh]
                        if hh == 0:
                            W(scalar, "sAMM", k - 1)
                        scalar.copy(w1c[k % 2][:, hh * 2048:(hh + 1) * 2048], src[:])
                    scalar.mul(w1c[k % 2][:, 0:1], w1c[k % 2][:, 0:1],
                               1.0).then_inc(S["sW1C"], 1)
                    if k % 2 == 0:
                        if k + 2 < SC:
                            for hh in range(2):
                                scalar.dma_start(
                                    w1stA[hh][:],
                                    bass.AP(w1, (k + 2) * P * D + hh * 2048,
                                            [[D, P], [1, 2048]])
                                ).then_inc(S["qscal"], 16)
                        if k + 1 < SC:
                            for km in (k, k + 1):
                                W(scalar, "sAE", km - 3)
                                scalar.dma_start(
                                    maskb[:, (km % 4) * COLS:(km % 4 + 1) * COLS],
                                    masks.ap()[km * P:(km + 1) * P, :]
                                ).then_inc(S["qscal"], 16)
                for j in range(NT + KOFF):
                    i = j - KOFF
                    if 0 <= i:
                        W(scalar, "sYMM", i + 1)
                        W(scalar, "sEP", i)
                        scalar.activation(gts[:], pM[i % 2][:], AF.Gelu)
                        scalar.mul(gts[:, 0:1], gts[:, 0:1], 1.0).then_inc(S["sGELU"], 1)
                    if j < NT:
                        if j == 0:
                            W(scalar, "sAE", SC)
                            W(scalar, "sWQG", 16)
                        W(scalar, "sST", j + 1)
                        W(scalar, "sTW", 4 * j)
                        for qq in range(4):
                            hx = 4 * j + qq
                            scalar.activation(tbt[:, qq * QW:(qq + 1) * QW],
                                              xq[hx % XR][:], AF.Identity,
                                              bias=nmw[:, (j % 2) * 64:(j % 2) * 64 + 1],
                                              scale=rsw[:, (j % 2) * 64:(j % 2) * 64 + 1])
                        scalar.mul(tbt[:, 0:1], tbt[:, 0:1], 1.0).then_inc(S["sTB"], 1)
                        for w in range(4):
                            idx = 4 * j + w
                            W(scalar, "sTW", idx + 1)
                            if w == 0:
                                W(scalar, "sYMM", j - 1)
                            scalar.copy(tbf[j % 2][:, w * 1024:(w + 1) * 1024],
                                        pTb[idx % 2][:])
                            scalar.mul(vjk[:, 0:1], vjk[:, 0:1], 1.0).then_inc(S["sTBE"], 1)
                        for w in range(4):
                            idx = 4 * j + w
                            W(scalar, "sTQw", idx + 1)
                            if w == 0:
                                W(scalar, "sFMM", j - 1)
                            scalar.copy(qf[j % 2][:, w * 1024:(w + 1) * 1024],
                                        pTq[idx % 2][:])
                            scalar.mul(vjk[:, 0:1], vjk[:, 0:1], 1.0).then_inc(S["sQE"], 1)

            @blk.vector
            def _(vector):
                def rne5(dst, src, b1, b2_, ph):
                    vector.tensor_scalar(b1, src, M64, -M64, op0=ALU.add, op1=ALU.add)
                    vector.tensor_scalar(b2_, src, ph, None, op0=ALU.mult)
                    vector.tensor_scalar(src, src, M32, -M32, op0=ALU.add, op1=ALU.add)
                    vector.tensor_tensor(b1, b1, src, op=ALU.add)
                    vector.tensor_tensor(dst, b1, b2_, op=ALU.subtract)

                def ep(i):
                    W(vector, "sFMM", i + 1)
                    vector.tensor_scalar(E1[:], pF[i % 2][:], INV256, None, op0=ALU.mult)
                    vector.tensor_scalar(E2[:], E1[:], M64, -M64, op0=ALU.add, op1=ALU.add)
                    vector.tensor_scalar(E3[:], E1[:], M32, -M32, op0=ALU.add, op1=ALU.add)
                    vector.tensor_tensor(E2[:], E2[:], E3[:], op=ALU.add)
                    vector.tensor_scalar(E3[:], E1[:], phiw[:, 0:1], None, op0=ALU.mult)
                    vector.tensor_tensor(E2[:], E2[:], E3[:], op=ALU.subtract)
                    vector.tensor_tensor(E3[:], E1[:], E2[:], op=ALU.is_lt)
                    vector.tensor_tensor(E2[:], E2[:], E3[:], op=ALU.subtract)
                    vector.tensor_scalar(E1[:], E2[:], QMIN, QMAX, op0=ALU.max, op1=ALU.min)
                    vector.tensor_tensor(E2[:], E1[:], bqrep[:], op=ALU.add)
                    vector.tensor_scalar(E1[:], E2[:], QMIN, QMAX, op0=ALU.max, op1=ALU.min)
                    vector.wait_ge(S["qsync"], QS[("xr", i)])
                    vector.scalar_tensor_tensor(E2[:], E1[:], INV256, xrs[i % 2][:],
                                                op0=ALU.mult, op1=ALU.add)
                    W(vector, "sGELU", i + 1)
                    if i >= 2:
                        vector.wait_ge(S["qvec"], 16 * (i - 1))
                    vector.tensor_tensor(osb[i % 2][:], E2[:], gts[:], op=ALU.add)
                    vector.tensor_copy(vjk[:, 0:32], vjk[:, 32:64]).then_inc(S["sEP"], 1)

                vector.memset(vjk[:], 0.0)
                vector.wait_ge(S["qscal"], QL["cs"])
                vector.tensor_scalar(phiw[:], cb(4, 64), M64, -M64, op0=ALU.add, op1=ALU.add)
                vector.wait_ge(S["qscal"], QL["b2s"])
                vector.tensor_copy(b2b[0:1, :], sb2[0:1, :])
                vector.wait_ge(S["qscal"], QL["bfs"])
                vector.tensor_scalar(E1[0:1, :], sbf[0:1, :], 256.0, QMIN,
                                     op0=ALU.mult, op1=ALU.max)
                vector.tensor_scalar(E1[0:1, :], E1[0:1, :], QMAX, None, op0=ALU.min)
                rne5(bqrow[0:1, :], E1[0:1, :], E2[0:1, :], E3[0:1, :], phiw[0:1, 0:1])
                vector.tensor_copy(vjk[:, 0:32], vjk[:, 32:64]).then_inc(S["sBQR"], 1)
                W(vector, "sBC", 1)
                vector.tensor_copy(bqrep[:], pF[1][:])
                vector.tensor_copy(vjk[:, 0:32], vjk[:, 32:64]).then_inc(S["sBCE"], 1)
                for c in range(16):
                    vector.wait_ge(S["qgps"], QG_WF[c])
                    W(vector, "sWQG", c - 1)
                    vector.tensor_scalar(wfcl[c % 2][:], wfst[c % 2][:], 256.0, QMIN,
                                         op0=ALU.mult, op1=ALU.max)
                    vector.tensor_scalar(wfcl[c % 2][:], wfcl[c % 2][:], QMAX, None,
                                         op0=ALU.min)
                    vector.tensor_copy(vjk[:, 0:32], vjk[:, 32:64]).then_inc(S["sWQC"], 1)
                for k in range(SC):
                    W(vector, "sAMM", k + 1)
                    vector.wait_ge(S["qscal"], QL[("mask", k)])
                    vector.tensor_tensor(attn[:, k * COLS:(k + 1) * COLS], pM[k % 2][:],
                                         maskb[:, (k % 4) * COLS:(k % 4 + 1) * COLS],
                                         op=ALU.mult)
                    vector.tensor_copy(vjk[:, 0:32], vjk[:, 32:64]).then_inc(S["sAE"], 1)

                for j in range(NT + KOFF):
                    if j < NT:
                        jc = (j % 2) * 64
                        for qq in range(4):
                            vector.wait_ge(S["qsync"], QS[("x", j, qq)])
                            for c in range(4):
                                cc = 4 * qq + c
                                vector.bn_stats(bn6[:, 6 * cc:6 * (cc + 1)],
                                                xq[(4 * j + qq) % XR][:, 256 * c:256 * (c + 1)])
                        vector.bn_aggr(mv[:], bn6[:])
                        vector.tensor_copy(vjk[:, 0:32], vjk[:, 32:64])
                        vector.tensor_scalar(vew[:], bass.AP(mv, 1, [[2, P], [0, 64]]),
                                             EPS, None, op0=ALU.add)
                        vector.tensor_scalar(rsw[:, jc:jc + 64], vew[:], -0.5, 1.5,
                                             op0=ALU.mult, op1=ALU.add)
                        for _ in range(3):
                            vector.tensor_tensor(nww[:], rsw[:, jc:jc + 64],
                                                 rsw[:, jc:jc + 64], op=ALU.mult)
                            vector.tensor_tensor(nww[:], nww[:], vew[:], op=ALU.mult)
                            vector.tensor_scalar(nww[:], nww[:], -0.5, 1.5,
                                                 op0=ALU.mult, op1=ALU.add)
                            vector.tensor_tensor(rsw[:, jc:jc + 64], rsw[:, jc:jc + 64],
                                                 nww[:], op=ALU.mult)
                        vector.tensor_tensor(nmp[:], bass.AP(mv, 0, [[2, P], [0, 64]]),
                                             rsw[:, jc:jc + 64], op=ALU.mult)
                        vector.tensor_scalar(aqw[:], rsw[:, jc:jc + 64], 256.0, None,
                                             op0=ALU.mult)
                        vector.tensor_scalar(nbw[:], nmp[:], 256.0, None, op0=ALU.mult)
                        vector.tensor_scalar(nmw[:, jc:jc + 64], nmp[:], -1.0, None,
                                             op0=ALU.mult)
                        W(vector, "sQG", j - 1)
                        a0, b0 = aqw[:, 0:1], nbw[:, 0:1]
                        vector.tensor_scalar(stV[:, 0:QW], xq[(4 * j) % XR][:], a0, b0,
                                             op0=ALU.mult, op1=ALU.subtract)
                        vector.tensor_scalar(stV[:, QW:2 * QW], xq[(4 * j + 1) % XR][:],
                                             a0, b0, op0=ALU.mult, op1=ALU.subtract)
                        vector.tensor_scalar(stV[:, 2 * QW:WV],
                                             xq[(4 * j + 2) % XR][:, 0:WV - 2 * QW],
                                             a0, b0, op0=ALU.mult, op1=ALU.subtract)
                        vector.tensor_scalar(stG[j % 2][:, 0:3 * QW - WV],
                                             xq[(4 * j + 2) % XR][:, WV - 2 * QW:QW],
                                             a0, b0, op0=ALU.mult, op1=ALU.subtract)
                        vector.tensor_scalar(stG[j % 2][:, 3 * QW - WV:WG],
                                             xq[(4 * j + 3) % XR][:],
                                             a0, b0, op0=ALU.mult, op1=ALU.subtract)
                        vector.tensor_copy(vjk[:, 0:32], vjk[:, 32:64]).then_inc(S["sST"], 1)
                        W(vector, "sTQw", 4 * (j - 1) + 3)
                        for h in range(2):
                            rne5(qt[:, h * HV:(h + 1) * HV], stV[:, h * HV:(h + 1) * HV],
                                 vc1[:], vc2[:], phiw[:, 0:1])
                        vector.tensor_copy(vjk[:, 0:32], vjk[:, 32:64]).then_inc(S["sQV"], 1)
                    i = j - KOFF
                    if 0 <= i:
                        ep(i)
                vector.wait_ge(S["qvec"], 16 * NT)

            @blk.gpsimd
            def _(g):
                def grne(dst, src, w, s1, s2):
                    g.tensor_tensor(s1[:, 0:w], src, cb(0, w), op=ALU.add)
                    g.tensor_tensor(s1[:, 0:w], s1[:, 0:w], cb(1, w), op=ALU.add)
                    g.tensor_tensor(s2[:, 0:w], src, cb(2, w), op=ALU.add)
                    g.tensor_tensor(s2[:, 0:w], s2[:, 0:w], cb(3, w), op=ALU.add)
                    g.tensor_tensor(s1[:, 0:w], s1[:, 0:w], s2[:, 0:w], op=ALU.add)
                    g.tensor_tensor(s2[:, 0:w], src, phib(w), op=ALU.mult)
                    g.tensor_tensor(dst, s1[:, 0:w], s2[:, 0:w], op=ALU.subtract)

                g.memset(gjk[:], 0.0)
                for c in range(2):
                    g.dma_start(wfst[c][:],
                                bass.AP(wfs, c * 2 * P * COLS,
                                        [[COLS, P], [P * COLS, 2], [1, COLS]])
                                ).then_inc(S["qgps"], 16)
                for c in range(16):
                    W(g, "sWQC", c + 1)
                    for sub in range(2):
                        jcc = 2 * c + sub
                        grne(wqT[:, jcc * COLS:(jcc + 1) * COLS],
                             wfcl[c % 2][:, sub * COLS:(sub + 1) * COLS], COLS, wqs1, wqs2)
                    g.tensor_copy(gjk[:, 0:32], gjk[:, 32:64]).then_inc(S["sWQG"], 1)
                    if c + 2 < 16:
                        g.dma_start(wfst[c % 2][:],
                                    bass.AP(wfs, (c + 2) * 2 * P * COLS,
                                            [[COLS, P], [P * COLS, 2], [1, COLS]])
                                    ).then_inc(S["qgps"], 16)
                for j in range(NT + KOFF):
                    if j < NT:
                        W(g, "sST", j + 1)
                        W(g, "sTQw", 4 * j)
                        grne(qt[:, WV:D], stG[j % 2][:], WG, gc1, gc2)
                        g.tensor_copy(gjk[:, 0:32], gjk[:, 32:64]).then_inc(S["sQG"], 1)
                    i = j - KOFF
                    if 0 <= i:
                        W(g, "sEP", i + 1)
                        g.dma_start(bass.AP(out, i * P * COLS, [[COLS, P], [1, COLS]]),
                                    osb[i % 2][:]).then_inc(S["qvec"], 16)
                g.wait_ge(S["qvec"], 16 * NT)

            @blk.tensor
            def _(tensor):
                tensor.wait_ge(S["qscal"], QL["onesk"])
                W(tensor, "sBQR", 1)
                tensor.matmul(pF[1][:], sones[0:1, :], bqrow[0:1, :],
                              start=True, stop=True).then_inc(S["sBC"], 1)
                for k in range(SC):
                    W(tensor, "sW2C", 1)
                    W(tensor, "sW1C", k + 1)
                    W(tensor, "sAE", k - 1)
                    for dc in range(SC):
                        mm = tensor.matmul(pM[k % 2][:], w1c[k % 2][:, dc * P:(dc + 1) * P],
                                           w2T[:, dc * COLS:(dc + 1) * COLS],
                                           start=(dc == 0), stop=(dc == SC - 1))
                    mm.then_inc(S["sAMM"], 1)
                for j in range(NT + KOFF):
                    i = j - KOFF
                    if 0 <= i:
                        W(tensor, "sTBE", 4 * (i + 1))
                        W(tensor, "sAE", SC)
                        W(tensor, "sGELU", i - 1)
                        for sc in range(SC):
                            tensor.matmul(pM[i % 2][:], tbf[i % 2][:, sc * P:(sc + 1) * P],
                                          attn[:, sc * COLS:(sc + 1) * COLS],
                                          start=(sc == 0), stop=False)
                        mm = tensor.matmul(pM[i % 2][:], onb[0:1, :], b2b[0:1, :],
                                           start=False, stop=True)
                        mm.then_inc(S["sYMM"], 1)
                        W(tensor, "sQE", 4 * (i + 1))
                        W(tensor, "sWQG", 16)
                        W(tensor, "sBCE", 1)
                        W(tensor, "sEP", i - 1)
                        for sc in range(SC):
                            mm = tensor.matmul(pF[i % 2][:], qf[i % 2][:, sc * P:(sc + 1) * P],
                                               wqT[:, sc * COLS:(sc + 1) * COLS],
                                               start=(sc == 0), stop=(sc == SC - 1))
                        mm.then_inc(S["sFMM"], 1)
                    if j < NT:
                        W(tensor, "sTB", j + 1)
                        for w in range(4):
                            idx = 4 * j + w
                            W(tensor, "sTBE", idx - 1)
                            for b in range(8):
                                sc = 8 * w + b
                                tt = tensor.transpose(pTb[idx % 2][:, b * P:(b + 1) * P],
                                                      tbt[:, sc * P:(sc + 1) * P], sidb[:])
                            tt.then_inc(S["sTW"], 1)
                        W(tensor, "sQV", j + 1)
                        W(tensor, "sQG", j + 1)
                        for w in range(4):
                            idx = 4 * j + w
                            W(tensor, "sQE", idx - 1)
                            for b in range(8):
                                sc = 8 * w + b
                                tt = tensor.transpose(pTq[idx % 2][:, b * P:(b + 1) * P],
                                                      qt[:, sc * P:(sc + 1) * P], sidh[:])
                            tt.then_inc(S["sTQw"], 1)

    return nc


def kernel(L0_attn_x, L0_attn_w1, L0_attn_w2, L0_attn_b2, L0_attn_sparse_mask,
           L0_ffn_weight, L0_ffn_bias):
    x = np.ascontiguousarray(np.asarray(L0_attn_x, dtype=np.float32).reshape(R, D))
    w1 = np.asarray(L0_attn_w1, dtype=np.float32)
    w2 = np.ascontiguousarray(np.asarray(L0_attn_w2, dtype=np.float32))
    b2 = np.asarray(L0_attn_b2, dtype=np.float32)
    mask = np.ascontiguousarray(np.asarray(L0_attn_sparse_mask, dtype=np.float32))
    wf = np.ascontiguousarray(np.asarray(L0_ffn_weight, dtype=np.float32))
    bf = np.asarray(L0_ffn_bias, dtype=np.float32)

    consts = np.zeros((P, 8), np.float32)
    consts[:, 0] = M64
    consts[:, 1] = -M64
    consts[:, 2] = M32
    consts[:, 3] = -M32
    consts[:, 4] = 1.0

    nc = build_nc()
    in_maps = []
    for c in range(8):
        cs_off = c * COLS
        cslice = slice(cs_off, cs_off + COLS)
        roll = np.r_[cs_off:D, 0:cs_off]
        xr = np.ascontiguousarray(x[:, roll])
        w1r = np.ascontiguousarray(w1[:, roll])
        w1r = np.ascontiguousarray(
            w1r.reshape(SC, P, SC, P).transpose(2, 1, 0, 3).reshape(SC, P, D))
        maskr = mask[roll, :][:, cslice]
        wfr = wf[cslice, :][:, roll]
        in_maps.append({
            "x_own": xr,
            "w1": w1r,
            "w2s": np.ascontiguousarray(w2[cslice, :].T),
            "masks": np.ascontiguousarray(maskr),
            "b2s": np.ascontiguousarray(b2[cslice].reshape(1, COLS)),
            "wfs": np.ascontiguousarray(wfr.T),
            "bfs": np.ascontiguousarray(bf[cslice].reshape(1, COLS)),
            "identb": np.eye(P, dtype=ml_dtypes.bfloat16),
            "identh": np.eye(P, dtype=np.float16),
            "onesk": np.ones((1, P), dtype=np.float32),
            "onesb": np.ones((1, P), dtype=ml_dtypes.bfloat16),
            "csd": consts,
        })
    res = run_bass_kernel_spmd(nc, in_maps, core_ids=list(range(8)))
    outs = [np.asarray(r["out"], dtype=np.float32) for r in res.results]
    full = np.concatenate(outs, axis=1)
    return full.reshape(2, 4096, 4096)
